# version 20
# baseline (speedup 1.0000x reference)
import numpy as np
import concourse.bass as bass
import concourse.mybir as mybir
from concourse.bass_utils import run_bass_kernel_spmd

F32 = mybir.dt.float32
BF16 = mybir.dt.bfloat16
I32 = mybir.dt.int32
AF = mybir.ActivationFunctionType
ALU = mybir.AluOpType

NCORES = 8
NTOK = 1024
D = 4096
DI = 11008
KT = 32
JT = 86
NPASS = 8
DTP = 4
HG = 43
DG = 22
DGS = [4] * 21 + [2]
HLR = 6
DPR = 4
XG = [2, 2, 4, 4, 4, 4, 4, 4, 4]
XG0 = [0, 2, 4, 8, 12, 16, 20, 24, 28]
XGOF = []
for _gi, (_k0, _kl) in enumerate(zip(XG0, XG)):
    XGOF += [_gi + 1] * _kl
EPS = 1e-5


def _dpan_consumed_val(gidx):
    pp, gp2 = divmod(gidx, DG)
    q_end = (4 * gp2 + DGS[gp2] - 1) // 2
    return pp * HG + q_end + 1


def build_program():
    nc = bass.Bass("TRN2", target_bir_lowering=False)

    xf = nc.dram_tensor("xf", [D, NTOK], F32, kind="ExternalInput").ap()
    wt = nc.dram_tensor("wt", [NPASS, DTP, 128, KT, 128], F32, kind="ExternalInput").ap()
    gp = nc.dram_tensor("gp", [JT, 128, KT, 128], I32, kind="ExternalInput").ap()
    up = nc.dram_tensor("up", [JT, 128, KT, 128], I32, kind="ExternalInput").ap()
    dw = nc.dram_tensor("dw", [NPASS, JT, 128, DTP, 128], I32, kind="ExternalInput").ap()
    bs = nc.dram_tensor("bs", [128, KT], F32, kind="ExternalInput").ap()
    scl = nc.dram_tensor("scl", [1, 3], F32, kind="ExternalInput").ap()
    yt = nc.dram_tensor("yt", [D, NTOK], F32, kind="ExternalOutput").ap()
    hbf = nc.dram_tensor("hbf", [JT, 128, NTOK], BF16, kind="Internal").ap()

    from contextlib import ExitStack
    es = ExitStack()
    sb = lambda name, shape, dt: es.enter_context(nc.sbuf_tensor(name, shape, dt))
    ps = lambda name, shape: es.enter_context(nc.psum_tensor(name, shape, F32))
    sem = lambda name: es.enter_context(nc.semaphore(name))
    xnT = sb("xnT", [128, KT * NTOK], BF16)
    xt = sb("xt", [128, 4 * NTOK], F32)
    m_bc = sb("m_bc", [128, NTOK], BF16)
    r_bc = sb("r_bc", [128, NTOK], BF16)
    srows = sb("srows", [1, 3 * NTOK], F32)
    gpan = sb("gpan", [128, 2 * KT * 128], BF16)
    upan = sb("upan", [128, 2 * KT * 128], BF16)
    silu_t = sb("silu_t", [128, NTOK], F32)
    hsb = sb("hsb", [128, 4 * NTOK], BF16)
    wch = sb("wch", [128, 2 * KT * 128], BF16)
    dpan = sb("dpan", [128, DPR * DTP * 512], BF16)
    hld = sb("hld", [128, HLR * 2 * NTOK], BF16)
    ost = sb("ost", [128, 2 * NTOK], F32)
    bias_sb = sb("bias_sb", [128, KT], F32)
    ones_col = sb("ones_col", [128, 1], BF16)
    ones_row = sb("ones_row", [1, 128], F32)
    sgc = sb("sgc", [128, 1], F32)
    susdc = sb("susdc", [128, 1], F32)
    scl_sb = sb("scl_sb", [1, 8], F32)
    eps_sb = sb("eps_sb", [1, 1], F32)
    psA = ps("psA", [128, 2048])
    psB = ps("psB", [128, 2048])

    s_cst = sem("s_cst")
    s_xa = [sem("s_xa%d" % i) for i in range(4)]
    s_xgp = [sem("s_xg%d" % i) for i in range(4)]
    s_wgp = [sem("s_wg0"), sem("s_wg1")]
    s_wup = [sem("s_wu0"), sem("s_wu1")]
    s_wcp = [sem("s_wc0"), sem("s_wc1")]
    s_wdp = [sem("s_wd%d" % i) for i in range(DPR)]
    s_hlp = [sem("s_hl%d" % i) for i in range(HLR)]
    s_hsp = [sem("s_hs0"), sem("s_hs1")]
    s_osp = [sem("s_os0"), sem("s_os1")]
    s_pb = sem("s_pb"); s_px = sem("s_px"); s_pq = sem("s_pq")
    s_pg = sem("s_pg"); s_pu = sem("s_pu"); s_pa = sem("s_pa")
    s_pdh = sem("s_pdh"); s_sq = sem("s_sq"); s_sq2 = sem("s_sq2")
    s_si = sem("s_si"); s_vc = sem("s_vc"); s_vst = sem("s_vst")
    s_st = sem("s_st"); s_hv = sem("s_hv"); s_ve = sem("s_ve")
    s_vch = sem("s_vch"); s_gch = sem("s_gch"); s_pde = sem("s_pde")
    s_xnv = sem("s_xnv"); s_xnp = sem("s_xnp"); s_mbc = sem("s_mbc"); s_wrm = sem("s_wrm")

    def rsem(sems, g):
        n = len(sems)
        return sems[g % n], 16 * (g // n + 1)

    def pdh_wait(gi):
        p, q = divmod(gi, HG)
        if q == HG - 1:
            return s_pde, DTP * (p + 1)
        return s_pdh, (HG - 1) * p + q + 1

    with es, nc.Block() as blk:
        def pslot(dt):
            return [psA[:, 0:1024], psA[:, 1024:2048],
                    psB[:, 0:1024], psB[:, 1024:2048]][dt]

        @blk.sync
        def _(sy):
            sy.dma_start(bias_sb[:, :], bs).then_inc(s_cst, 16)
            sy.dma_start(scl_sb[0:1, 0:3], scl).then_inc(s_cst, 16)
            for p in range(NPASS):
                for q in range(HG):
                    gi = p * HG + q
                    if gi == 0:
                        sy.wait_ge(s_hsp[0], 16 * 22)
                        sy.wait_ge(s_hsp[1], 16 * 21)
                    if gi >= HLR:
                        sy.wait_ge(*pdh_wait(gi - HLR))
                    sl = (gi % HLR) * 2 * NTOK
                    sy.dma_start(
                        hld[:, sl:sl + 2 * NTOK].rearrange("p (j n) -> p j n", j=2),
                        hbf[2 * q:2 * q + 2].rearrange("j p n -> p j n"),
                    ).then_inc(s_hlp[gi % HLR], 16)
                for dt in range(DTP):
                    i = p * DTP + dt
                    if i < 4:
                        sy.wait_ge(s_xnv, len(XG))
                    else:
                        sy.wait_ge(s_ve, i - 3)
                    r0 = (p * DTP + dt) * 128
                    sy.dma_start(
                        xt[:, (i % 4) * NTOK:(i % 4 + 1) * NTOK],
                        xf[r0:r0 + 128, :],
                    ).then_inc(s_xgp[i % 4], 16)

        @blk.gpsimd
        def _(g):
            def xn_load(gq):
                if gq >= 4:
                    g.wait_ge(s_px, 4 * (gq - 4) + 4)
                    g.wait_ge(s_sq, 4 * (gq - 4) + 4)
                k0 = gq * 4
                g.dma_start(
                    xnT[:, k0 * NTOK:(k0 + 4) * NTOK]
                    .rearrange("p (k n) -> p k n", k=4),
                    xf[k0 * 128:(k0 + 4) * 128, :]
                    .rearrange("(k p) n -> p k n", p=128),
                ).then_inc(s_xa[gq % 4], 16)

            def panel_load(jt):
                sl = (jt % 2) * KT * 128
                if jt >= 2:
                    g.wait_ge(s_pg, jt - 1)
                g.dma_start(
                    gpan[:, sl:sl + KT * 128],
                    gp[jt].rearrange("p k m -> p (k m)"),
                ).then_inc(s_wgp[jt % 2], 16)
                if jt >= 2:
                    g.wait_ge(s_pu, jt - 1)
                g.dma_start(
                    upan[:, sl:sl + KT * 128],
                    up[jt].rearrange("p k m -> p (k m)"),
                ).then_inc(s_wup[jt % 2], 16)

            def wch_load(p, dt):
                i = p * DTP + dt
                sl = (i % 2) * KT * 128
                if i >= 2:
                    g.wait_ge(s_pa, i - 1)
                g.dma_start(
                    wch[:, sl:sl + KT * 128],
                    wt[p, dt].rearrange("p k m -> p (k m)"),
                ).then_inc(s_wcp[i % 2], 16)

            def dpan_load(p, gg):
                gs = DGS[gg]
                gidx = p * DG + gg
                if gidx >= DPR:
                    pp, gp2 = divmod(gidx - DPR, DG)
                    q_end = (4 * gp2 + DGS[gp2] - 1) // 2
                    g.wait_ge(*pdh_wait(pp * HG + q_end))
                sl = (gidx % DPR) * DTP * 512
                g.dma_start(
                    dpan[:, sl:sl + gs * 512].rearrange("p (j m) -> p j m", j=gs),
                    dw[p, 4 * gg:4 * gg + gs].rearrange("j p d m -> p j (d m)"),
                ).then_inc(s_wdp[gidx % DPR], 16)

            for gq in range(KT // 4):
                xn_load(gq)
            panel_load(0)
            panel_load(1)
            for jt in range(2, JT - 2):
                panel_load(jt)
            wch_load(0, 0)
            wch_load(0, 1)
            dpan_load(0, 0)
            panel_load(JT - 2)
            panel_load(JT - 1)
            for p in range(NPASS):
                for dt in range(DTP):
                    if p == 0 and dt < 2:
                        continue
                    wch_load(p, dt)
                for gg in range(3):
                    if p == 0 and gg == 0:
                        continue
                    dpan_load(p, gg)
                for gg in range(3, DG):
                    dpan_load(p, gg)

        @blk.tensor
        def _(t):
            t.wait_ge(s_wrm, 1)
            for _ in range(24):
                t.matmul(psA[:, 0:512], hld[:, 8192:8320], hld[:, 8320:8832],
                         start=True, stop=True)
            t.wait_ge(s_vc, 1)
            def sqmms(kt):
                t.wait_ge(s_sq, kt + 1)
                ox = (kt % 8) * NTOK
                t.matmul(psA[0:1, 1024:1536], ones_col[:, :], hld[:, ox:ox + 512],
                         start=(kt == 0), stop=(kt == KT - 1))
                t.matmul(psA[0:1, 1536:2048], ones_col[:, :], hld[:, ox + 512:ox + 1024],
                         start=(kt == 0), stop=(kt == KT - 1)).then_inc(s_pq, 1)

            for kt in range(KT):
                t.wait_ge(*rsem(s_xa, kt // 4))
                o = kt * NTOK
                t.matmul(psA[0:1, 0:512], ones_col[:, :], xnT[:, o:o + 512],
                         start=(kt == 0), stop=(kt == KT - 1))
                t.matmul(psA[0:1, 512:1024], ones_col[:, :], xnT[:, o + 512:o + 1024],
                         start=(kt == 0), stop=(kt == KT - 1)).then_inc(s_px, 1)
                if kt >= 1:
                    sqmms(kt - 1)
            sqmms(KT - 1)
            t.wait_ge(s_vc, 2)
            t.matmul(psB[:, 0:2], ones_row[0:1, :], scl_sb[0:1, 4:6],
                     start=True, stop=True).then_inc(s_pb, 1)
            t.wait_ge(s_vst, 1)
            t.wait_ge(s_vc, 3)
            t.matmul(psB[:, 0:512], ones_row[0:1, :], srows[0:1, 0:512],
                     start=True, stop=True)
            t.matmul(psB[:, 512:1024], ones_row[0:1, :], srows[0:1, 512:1024],
                     start=True, stop=True).then_inc(s_pb, 1)
            t.wait_ge(s_st, 1)
            t.matmul(psB[:, 1024:1536], ones_row[0:1, :], srows[0:1, 2048:2560],
                     start=True, stop=True)
            t.matmul(psB[:, 1536:2048], ones_row[0:1, :], srows[0:1, 2560:3072],
                     start=True, stop=True).then_inc(s_pb, 1)
            t.wait_ge(*rsem(s_wgp, 0))
            t.wait_ge(*rsem(s_wgp, 1))
            lastg = [None, None]
            for gi, (k0g, klg) in enumerate(zip(XG0, XG)):
                t.wait_ge(s_xnv, gi + 1)
                for kt in range(k0g, k0g + klg):
                    for jt in (0, 1):
                        S = psA if jt == 0 else psB
                        pan = (jt % 2) * KT * 128
                        lhs = gpan[:, pan + kt * 128:pan + (kt + 1) * 128]
                        t.matmul(S[:, 0:512], lhs, xnT[:, kt * NTOK:kt * NTOK + 512],
                                 start=(kt == 0), stop=(kt == KT - 1))
                        lastg[jt] = t.matmul(
                            S[:, 512:1024], lhs,
                            xnT[:, kt * NTOK + 512:kt * NTOK + 1024],
                            start=(kt == 0), stop=(kt == KT - 1))
            lastg[0].then_inc(s_pg, 1)
            lastg[1].then_inc(s_pg, 1)
            for jt in (0, 1):
                S = psA if jt == 0 else psB
                pan = (jt % 2) * KT * 128
                t.wait_ge(*rsem(s_wup, jt))
                for kt in range(KT):
                    lhs = upan[:, pan + kt * 128:pan + (kt + 1) * 128]
                    t.matmul(S[:, 1024:1536], lhs, xnT[:, kt * NTOK:kt * NTOK + 512],
                             start=(kt == 0), stop=(kt == KT - 1))
                    i2 = t.matmul(S[:, 1536:2048], lhs,
                                  xnT[:, kt * NTOK + 512:kt * NTOK + 1024],
                                  start=(kt == 0), stop=(kt == KT - 1))
                i2.then_inc(s_pu, 1)
            for jt in range(2, JT):
                S = psA if jt % 2 == 0 else psB
                pan = (jt % 2) * KT * 128
                t.wait_ge(*rsem(s_wgp, jt))
                t.wait_ge(s_si, jt - 1)
                t.wait_ge(s_hv, jt - 1)
                for kt in range(KT):
                    lhs = gpan[:, pan + kt * 128:pan + (kt + 1) * 128]
                    t.matmul(S[:, 0:512], lhs, xnT[:, kt * NTOK:kt * NTOK + 512],
                             start=(kt == 0), stop=(kt == KT - 1))
                    i2 = t.matmul(S[:, 512:1024], lhs,
                                  xnT[:, kt * NTOK + 512:kt * NTOK + 1024],
                                  start=(kt == 0), stop=(kt == KT - 1))
                i2.then_inc(s_pg, 1)
                t.wait_ge(*rsem(s_wup, jt))
                for kt in range(KT):
                    lhs = upan[:, pan + kt * 128:pan + (kt + 1) * 128]
                    t.matmul(S[:, 1024:1536], lhs, xnT[:, kt * NTOK:kt * NTOK + 512],
                             start=(kt == 0), stop=(kt == KT - 1))
                    i2 = t.matmul(S[:, 1536:2048], lhs,
                                  xnT[:, kt * NTOK + 512:kt * NTOK + 1024],
                                  start=(kt == 0), stop=(kt == KT - 1))
                i2.then_inc(s_pu, 1)
            t.wait_ge(s_si, JT)
            t.wait_ge(s_hv, JT)
            for p in range(NPASS):
                for dt in range(DTP):
                    i = p * DTP + dt
                    S = pslot(dt)
                    t.wait_ge(*rsem(s_wcp, i))
                    if p > 0:
                        t.wait_ge(s_ve, (p - 1) * DTP + dt + 1)
                    pan = (i % 2) * KT * 128
                    for kt in range(KT):
                        lhs = wch[:, pan + kt * 128:pan + (kt + 1) * 128]
                        t.matmul(S[:, 0:512], lhs, xnT[:, kt * NTOK:kt * NTOK + 512],
                                 start=(kt == 0), stop=False)
                        i2 = t.matmul(S[:, 512:1024], lhs,
                                      xnT[:, kt * NTOK + 512:kt * NTOK + 1024],
                                      start=(kt == 0), stop=False)
                    i2.then_inc(s_pa, 1)
                for jt in range(JT):
                    g4, j4 = divmod(jt, 4)
                    q2, j2 = divmod(jt, 2)
                    gidx = p * DG + g4
                    gi = p * HG + q2
                    if j4 == 0:
                        t.wait_ge(*rsem(s_wdp, gidx))
                    if j2 == 0:
                        t.wait_ge(*rsem(s_hlp, gi))
                    dsl = (gidx % DPR) * DTP * 512
                    hsl = (gi % HLR) * 2 * NTOK + j2 * NTOK
                    last = jt == JT - 1
                    for dt in range(DTP):
                        S = pslot(dt)
                        lhs = dpan[:, dsl + (j4 * DTP + dt) * 128:
                                   dsl + (j4 * DTP + dt + 1) * 128]
                        t.matmul(S[:, 0:512], lhs, hld[:, hsl:hsl + 512],
                                 start=False, stop=last)
                        i2 = t.matmul(S[:, 512:1024], lhs, hld[:, hsl + 512:hsl + 1024],
                                      start=False, stop=last)
                        if last:
                            i2.then_inc(s_pde, 1)
                    if j2 == 1 and not last:
                        i2.then_inc(s_pdh, 1)

        @blk.vector
        def _(v):
            v.memset(hld[:, 8192:8832], 1.0).then_inc(s_wrm, 1)
            v.memset(ones_col[:, :], 1.0)
            v.memset(eps_sb[0:1, :], EPS)
            v.memset(ones_row[0:1, :], 1.0).then_inc(s_vc, 1)
            v.wait_ge(s_cst, 32)
            v.tensor_copy(scl_sb[0:1, 4:5], scl_sb[0:1, 0:1])
            v.tensor_mul(scl_sb[0:1, 5:6], scl_sb[0:1, 1:2], scl_sb[0:1, 2:3]).then_inc(s_vc, 1)
            vch = [0]

            def vedge(inst):
                vch[0] += 1
                inst.then_inc(s_vch, 1)
                return vch[0]

            for kt in range(KT):
                v.wait_ge(*rsem(s_xa, kt // 4))
                if kt >= 8:
                    v.wait_ge(s_pq, kt - 7)
                o = kt * NTOK
                ox = (kt % 8) * NTOK
                v.tensor_mul(hld[:, ox:ox + NTOK], xnT[:, o:o + NTOK],
                             xnT[:, o:o + NTOK]).then_inc(s_sq, 1)
            v.wait_ge(s_px, KT)
            v.wait_ge(s_pq, KT)
            vedge(v.tensor_scalar_mul(srows[0:1, 0:1024], psA[0:1, 0:1024], 1.0 / D))
            n2 = vedge(v.tensor_scalar_mul(srows[0:1, 1024:2048], psA[0:1, 1024:2048], 1.0 / D))
            v.wait_ge(s_vch, n2)
            n3 = vedge(v.tensor_mul(srows[0:1, 2048:3072], srows[0:1, 0:1024], srows[0:1, 0:1024]))
            v.wait_ge(s_vch, n3)
            v.tensor_sub(ost[0:1, 0:1024], srows[0:1, 1024:2048],
                         srows[0:1, 2048:3072]).then_inc(s_vst, 1)
            v.wait_ge(s_pb, 1)
            v.tensor_copy(sgc[:, :], psB[:, 0:1])
            v.tensor_copy(susdc[:, :], psB[:, 1:2]).then_inc(s_vc, 1)
            v.wait_ge(s_pb, 2)
            nmc = vedge(v.tensor_copy(m_bc[:, :], psB[:, 0:1024]))
            v.wait_ge(s_vch, nmc)
            n = vedge(v.tensor_sub(
                hld[:, 0:XG[0] * NTOK].rearrange("p (k n) -> p k n", k=XG[0]),
                xnT[:, 0:XG[0] * NTOK].rearrange("p (k n) -> p k n", k=XG[0]),
                m_bc[:, :].unsqueeze(1).broadcast_to([128, XG[0], NTOK])))
            v.wait_ge(s_pb, 3)
            v.tensor_copy(r_bc[:, :], psB[:, 1024:2048]).then_inc(s_mbc, 1)
            v.wait_ge(s_mbc, 1)
            v.wait_ge(s_vch, n)
            v.tensor_mul(
                xnT[:, 0:XG[0] * NTOK].rearrange("p (k n) -> p k n", k=XG[0]),
                hld[:, 0:XG[0] * NTOK].rearrange("p (k n) -> p k n", k=XG[0]),
                r_bc[:, :].unsqueeze(1).broadcast_to([128, XG[0], NTOK])).then_inc(s_xnv, 1)
            for pr in range(1, len(XG)):
                k0, kl = XG0[pr], XG[pr]
                v.wait_ge(s_xnv, pr)
                n = vedge(v.tensor_sub(
                    hld[:, 0:kl * NTOK].rearrange("p (k n) -> p k n", k=kl),
                    xnT[:, k0 * NTOK:(k0 + kl) * NTOK].rearrange("p (k n) -> p k n", k=kl),
                    m_bc[:, :].unsqueeze(1).broadcast_to([128, kl, NTOK])))
                v.wait_ge(s_vch, n)
                v.tensor_mul(
                    xnT[:, k0 * NTOK:(k0 + kl) * NTOK].rearrange("p (k n) -> p k n", k=kl),
                    hld[:, 0:kl * NTOK].rearrange("p (k n) -> p k n", k=kl),
                    r_bc[:, :].unsqueeze(1).broadcast_to([128, kl, NTOK])).then_inc(s_xnv, 1)
            v.wait_ge(s_vc, 3)
            for jt in range(JT):
                S = psA if jt % 2 == 0 else psB
                v.wait_ge(s_pu, jt + 1)
                v.wait_ge(s_si, jt + 1)
                if jt >= 4:
                    v.wait_ge(*rsem(s_hsp, (jt - 4) // 2))
                v.scalar_tensor_tensor(
                    hsb[:, (jt % 4) * NTOK:(jt % 4 + 1) * NTOK],
                    S[:, 1024:2048], susdc[:, :],
                    silu_t[:, 0:NTOK],
                    ALU.mult, ALU.mult,
                ).then_inc(s_hv, 1)
            for p in range(NPASS):
                for dt in range(DTP):
                    i = p * DTP + dt
                    gxt = 2 * KT + i
                    v.wait_ge(s_pde, p * DTP + dt + 1)
                    v.wait_ge(*rsem(s_xgp, i))
                    if i in (30, 31):
                        dst = silu_t[:, 0:NTOK] if i == 30 else hsb[:, 0:2048].bitcast(F32)
                    else:
                        if i >= 2:
                            v.wait_ge(*rsem(s_osp, i - 2))
                        dst = ost[:, (i % 2) * NTOK:(i % 2 + 1) * NTOK]
                    v.scalar_tensor_tensor(
                        dst,
                        pslot(dt), bias_sb[:, p * DTP + dt:p * DTP + dt + 1],
                        xt[:, (gxt % 4) * NTOK:(gxt % 4 + 1) * NTOK],
                        ALU.add, ALU.add,
                    ).then_inc(s_ve, 1)

        @blk.scalar
        def _(s):
            s.wait_ge(s_vst, 1)
            s.wait_ge(s_vc, 1)
            s.activation(srows[0:1, 1024:2048], ost[0:1, 0:1024],
                         AF.Ln, bias=eps_sb[0:1, :]).then_inc(s_sq2, 1)
            s.wait_ge(s_sq2, 1)
            s.activation(srows[0:1, 2048:3072], srows[0:1, 1024:2048],
                         AF.Exp, scale=-0.5).then_inc(s_st, 1)
            s.wait_ge(s_vc, 3)
            for jt in range(JT):
                S = psA if jt % 2 == 0 else psB
                s.wait_ge(s_pg, jt + 1)
                if jt >= 1:
                    s.wait_ge(s_hv, jt)
                s.activation(silu_t[:, 0:NTOK],
                             S[:, 0:1024], AF.Silu, scale=sgc[:, :]).then_inc(s_si, 1)
                if jt % 2 == 1:
                    q = jt // 2
                    s.wait_ge(s_hv, jt + 1)
                    s.dma_start(
                        hbf[2 * q:2 * q + 2].rearrange("j p n -> p j n"),
                        hsb[:, ((2 * q) % 4) * NTOK:((2 * q) % 4 + 2) * NTOK]
                        .rearrange("p (j n) -> p j n", j=2),
                    ).then_inc(s_hsp[q % 2], 16)
            for p in range(NPASS):
                for dt in range(DTP):
                    i = p * DTP + dt
                    s.wait_ge(s_ve, i + 1)
                    r0 = (p * DTP + dt) * 128
                    if i == 30:
                        src = silu_t[:, 0:NTOK]
                    elif i == 31:
                        src = hsb[:, 0:2048].bitcast(F32)
                    else:
                        src = ost[:, (i % 2) * NTOK:(i % 2 + 1) * NTOK]
                    s.dma_start(
                        yt[r0:r0 + 128, :], src,
                    ).then_inc(s_osp[i % 2], 16)
            s.wait_ge(s_osp[0], 16 * (NPASS * DTP // 2))
            s.wait_ge(s_osp[1], 16 * (NPASS * DTP // 2))

    return nc


_NC_CACHE = {}


def _get_program():
    if "nc" not in _NC_CACHE:
        _NC_CACHE["nc"] = build_program()
    return _NC_CACHE["nc"]


def kernel(L0_attn_x, L0_attn_weight, L0_attn_bias, L0_ffn_gate_w, L0_ffn_gate_s,
           L0_ffn_up_w, L0_ffn_up_s, L0_ffn_down_w, L0_ffn_down_s):
    x = np.ascontiguousarray(np.asarray(L0_attn_x, dtype=np.float32).reshape(NCORES * NTOK, D))
    W = np.asarray(L0_attn_weight, dtype=np.float32)
    b = np.asarray(L0_attn_bias, dtype=np.float32)
    G = np.asarray(L0_ffn_gate_w, dtype=np.int32)
    U = np.asarray(L0_ffn_up_w, dtype=np.int32)
    Dw = np.asarray(L0_ffn_down_w, dtype=np.int32)

    wt = np.ascontiguousarray(W.reshape(NPASS, DTP, 128, KT, 128).transpose(0, 1, 4, 3, 2))
    gph = np.ascontiguousarray(G.reshape(JT, 128, KT, 128).transpose(0, 3, 2, 1))
    uph = np.ascontiguousarray(U.reshape(JT, 128, KT, 128).transpose(0, 3, 2, 1))
    dwh = np.ascontiguousarray(Dw.reshape(NPASS, DTP, 128, JT, 128).transpose(0, 3, 4, 1, 2))
    bsh = np.ascontiguousarray(b.reshape(KT, 128).T)
    sclh = np.ascontiguousarray(np.stack(
        [np.asarray(L0_ffn_gate_s, np.float32).reshape(()),
         np.asarray(L0_ffn_up_s, np.float32).reshape(()),
         np.asarray(L0_ffn_down_s, np.float32).reshape(())]).reshape(1, 3))

    nc = _get_program()
    in_maps = []
    for c in range(NCORES):
        xf_c = np.ascontiguousarray(x[c * NTOK:(c + 1) * NTOK, :].T)
        in_maps.append({"xf": xf_c, "wt": wt, "gp": gph, "up": uph,
                        "dw": dwh, "bs": bsh, "scl": sclh})

    res = run_bass_kernel_spmd(nc, in_maps, core_ids=list(range(NCORES)))

    out = np.empty((NCORES * NTOK, D), dtype=np.float32)
    for c in range(NCORES):
        out[c * NTOK:(c + 1) * NTOK, :] = res.results[c]["yt"].T
    return out.reshape(4, 2048, D)


# version 22
# speedup vs baseline: 1.1936x; 1.1936x over previous
import numpy as np
import concourse.bass as bass
import concourse.mybir as mybir
from concourse.bass_utils import run_bass_kernel_spmd

F32 = mybir.dt.float32
BF16 = mybir.dt.bfloat16
I32 = mybir.dt.int32
AF = mybir.ActivationFunctionType
ALU = mybir.AluOpType

NCORES = 8
NTOK = 1024
D = 4096
DI = 11008
KT = 32
JT = 86
NPASS = 8
DTP = 4
HG = 43
DG = 22
DGS = [4] * 21 + [2]
HLR = 6
DPR = 4
XG = [2, 2, 4, 4, 4, 4, 4, 4, 4]
XG0 = [0, 2, 4, 8, 12, 16, 20, 24, 28]
XGOF = []
for _gi, (_k0, _kl) in enumerate(zip(XG0, XG)):
    XGOF += [_gi + 1] * _kl
EPS = 1e-5


def _dpan_consumed_val(gidx):
    pp, gp2 = divmod(gidx, DG)
    q_end = (4 * gp2 + DGS[gp2] - 1) // 2
    return pp * HG + q_end + 1


def build_program():
    nc = bass.Bass("TRN2", target_bir_lowering=False)

    xf = nc.dram_tensor("xf", [D, NTOK], F32, kind="ExternalInput").ap()
    wt = nc.dram_tensor("wt", [NPASS, DTP, 128, KT, 128], F32, kind="ExternalInput").ap()
    gp = nc.dram_tensor("gp", [JT, 128, KT, 128], I32, kind="ExternalInput").ap()
    up = nc.dram_tensor("up", [JT, 128, KT, 128], I32, kind="ExternalInput").ap()
    dw = nc.dram_tensor("dw", [NPASS, JT, 128, DTP, 128], I32, kind="ExternalInput").ap()
    bs = nc.dram_tensor("bs", [128, KT], F32, kind="ExternalInput").ap()
    scl = nc.dram_tensor("scl", [1, 3], F32, kind="ExternalInput").ap()
    yt = nc.dram_tensor("yt", [D, NTOK], F32, kind="ExternalOutput").ap()
    hbf = nc.dram_tensor("hbf", [JT, 128, NTOK], BF16, kind="Internal").ap()

    from contextlib import ExitStack
    es = ExitStack()
    sb = lambda name, shape, dt: es.enter_context(nc.sbuf_tensor(name, shape, dt))
    ps = lambda name, shape: es.enter_context(nc.psum_tensor(name, shape, F32))
    sem = lambda name: es.enter_context(nc.semaphore(name))
    xnT = sb("xnT", [128, KT * NTOK], BF16)
    xt = sb("xt", [128, 4 * NTOK], F32)
    m_bc = sb("m_bc", [128, NTOK], BF16)
    r_bc = sb("r_bc", [128, NTOK], BF16)
    srows = sb("srows", [1, 3 * NTOK], F32)
    gpan = sb("gpan", [128, 2 * KT * 128], BF16)
    upan = sb("upan", [128, 2 * KT * 128], BF16)
    silu_t = sb("silu_t", [128, NTOK], F32)
    hsb = sb("hsb", [128, 4 * NTOK], BF16)
    wch = sb("wch", [128, 2 * KT * 128], BF16)
    dpan = sb("dpan", [128, DPR * DTP * 512], BF16)
    hld = sb("hld", [128, HLR * 2 * NTOK], BF16)
    ost = sb("ost", [128, 2 * NTOK], F32)
    bias_sb = sb("bias_sb", [128, KT], F32)
    ones_col = sb("ones_col", [128, 1], BF16)
    ones_row = sb("ones_row", [1, 128], F32)
    sgc = sb("sgc", [128, 1], F32)
    susdc = sb("susdc", [128, 1], F32)
    scl_sb = sb("scl_sb", [1, 8], F32)
    eps_sb = sb("eps_sb", [1, 1], F32)
    psA = ps("psA", [128, 2048])
    psB = ps("psB", [128, 2048])

    s_cst = sem("s_cst")
    s_xa = [sem("s_xa%d" % i) for i in range(4)]
    s_xgp = [sem("s_xg%d" % i) for i in range(4)]
    s_wgp = [sem("s_wg0"), sem("s_wg1")]
    s_wup = [sem("s_wu0"), sem("s_wu1")]
    s_wcp = [sem("s_wc0"), sem("s_wc1")]
    s_wdp = [sem("s_wd%d" % i) for i in range(DPR)]
    s_hlp = [sem("s_hl%d" % i) for i in range(HLR)]
    s_hsp = [sem("s_hs0"), sem("s_hs1")]
    s_osp = [sem("s_os0"), sem("s_os1")]
    s_pb = sem("s_pb"); s_px = sem("s_px"); s_pq = sem("s_pq")
    s_pg = sem("s_pg"); s_pu = sem("s_pu"); s_pa = sem("s_pa")
    s_pdh = sem("s_pdh"); s_sq = sem("s_sq"); s_sq2 = sem("s_sq2")
    s_si = sem("s_si"); s_vc = sem("s_vc"); s_vst = sem("s_vst")
    s_st = sem("s_st"); s_hv = sem("s_hv"); s_ve = sem("s_ve")
    s_vch = sem("s_vch"); s_gch = sem("s_gch"); s_pde = sem("s_pde")
    s_xnv = sem("s_xnv"); s_xnp = sem("s_xnp"); s_mbc = sem("s_mbc"); s_wrm = sem("s_wrm")

    def rsem(sems, g):
        n = len(sems)
        return sems[g % n], 16 * (g // n + 1)

    def pdh_wait(gi):
        p, q = divmod(gi, HG)
        if q == HG - 1:
            return s_pde, DTP * (p + 1)
        return s_pdh, (HG - 1) * p + q + 1

    with es, nc.Block() as blk:
        def pslot(dt):
            return [psA[:, 0:1024], psA[:, 1024:2048],
                    psB[:, 0:1024], psB[:, 1024:2048]][dt]

        @blk.sync
        def _(sy):
            sy.dma_start(bias_sb[:, :], bs).then_inc(s_cst, 16)
            sy.dma_start(scl_sb[0:1, 0:3], scl).then_inc(s_cst, 16)
            for p in range(NPASS):
                for q in range(HG):
                    gi = p * HG + q
                    if gi == 0:
                        sy.wait_ge(s_hsp[0], 16 * 22)
                        sy.wait_ge(s_hsp[1], 16 * 21)
                    if gi >= HLR:
                        sy.wait_ge(*pdh_wait(gi - HLR))
                    sl = (gi % HLR) * 2 * NTOK
                    sy.dma_start(
                        hld[:, sl:sl + 2 * NTOK].rearrange("p (j n) -> p j n", j=2),
                        hbf[2 * q:2 * q + 2].rearrange("j p n -> p j n"),
                    ).then_inc(s_hlp[gi % HLR], 16)
                for dt in range(DTP):
                    i = p * DTP + dt
                    if i < 4:
                        sy.wait_ge(s_xnv, len(XG))
                    else:
                        sy.wait_ge(s_ve, i - 3)
                    r0 = (p * DTP + dt) * 128
                    sy.dma_start(
                        xt[:, (i % 4) * NTOK:(i % 4 + 1) * NTOK],
                        xf[r0:r0 + 128, :],
                    ).then_inc(s_xgp[i % 4], 16)

        @blk.gpsimd
        def _(g):
            def xn_load(gq):
                for h in range(2):
                    k0 = gq * 8 + h * 4
                    g.dma_start(
                        xnT[:, k0 * NTOK:(k0 + 4) * NTOK]
                        .rearrange("p (k n) -> p k n", k=4),
                        xf[k0 * 128:(k0 + 4) * 128, :]
                        .rearrange("(k p) n -> p k n", p=128),
                    ).then_inc(s_xa[gq], 16)

            def panel_load(jt):
                sl = (jt % 2) * KT * 128
                if jt >= 2:
                    g.wait_ge(s_pg, jt - 1)
                g.dma_start(
                    gpan[:, sl:sl + KT * 128],
                    gp[jt].rearrange("p k m -> p (k m)"),
                ).then_inc(s_wgp[jt % 2], 16)
                if jt >= 2:
                    g.wait_ge(s_pu, jt - 1)
                g.dma_start(
                    upan[:, sl:sl + KT * 128],
                    up[jt].rearrange("p k m -> p (k m)"),
                ).then_inc(s_wup[jt % 2], 16)

            def wch_load(p, dt):
                i = p * DTP + dt
                sl = (i % 2) * KT * 128
                if i >= 2:
                    g.wait_ge(s_pa, i - 1)
                g.dma_start(
                    wch[:, sl:sl + KT * 128],
                    wt[p, dt].rearrange("p k m -> p (k m)"),
                ).then_inc(s_wcp[i % 2], 16)

            def dpan_load(p, gg):
                gs = DGS[gg]
                gidx = p * DG + gg
                if gidx >= DPR:
                    pp, gp2 = divmod(gidx - DPR, DG)
                    q_end = (4 * gp2 + DGS[gp2] - 1) // 2
                    g.wait_ge(*pdh_wait(pp * HG + q_end))
                sl = (gidx % DPR) * DTP * 512
                g.dma_start(
                    dpan[:, sl:sl + gs * 512].rearrange("p (j m) -> p j m", j=gs),
                    dw[p, 4 * gg:4 * gg + gs].rearrange("j p d m -> p j (d m)"),
                ).then_inc(s_wdp[gidx % DPR], 16)

            for gq in range(4):
                xn_load(gq)
            panel_load(0)
            panel_load(1)
            for jt in range(2, JT - 2):
                panel_load(jt)
            wch_load(0, 0)
            wch_load(0, 1)
            dpan_load(0, 0)
            panel_load(JT - 2)
            panel_load(JT - 1)
            for p in range(NPASS):
                for dt in range(DTP):
                    if p == 0 and dt < 2:
                        continue
                    wch_load(p, dt)
                for gg in range(3):
                    if p == 0 and gg == 0:
                        continue
                    dpan_load(p, gg)
                for gg in range(3, DG):
                    dpan_load(p, gg)

        @blk.tensor
        def _(t):
            t.wait_ge(s_wrm, 1)
            for _ in range(24):
                t.matmul(psA[:, 0:512], hld[:, 8192:8320], hld[:, 8320:8832],
                         start=True, stop=True)
            t.wait_ge(s_vc, 1)
            def sqmms(kt):
                t.wait_ge(s_sq, kt + 1)
                ox = (kt % 8) * NTOK
                t.matmul(psA[0:1, 1024:1536], ones_col[:, :], hld[:, ox:ox + 512],
                         start=(kt == 0), stop=(kt == KT - 1))
                t.matmul(psA[0:1, 1536:2048], ones_col[:, :], hld[:, ox + 512:ox + 1024],
                         start=(kt == 0), stop=(kt == KT - 1)).then_inc(s_pq, 1)

            for kt in range(KT):
                t.wait_ge(s_xa[kt // 8], 32)
                o = kt * NTOK
                t.matmul(psA[0:1, 0:512], ones_col[:, :], xnT[:, o:o + 512],
                         start=(kt == 0), stop=(kt == KT - 1))
                t.matmul(psA[0:1, 512:1024], ones_col[:, :], xnT[:, o + 512:o + 1024],
                         start=(kt == 0), stop=(kt == KT - 1)).then_inc(s_px, 1)
                if kt >= 1:
                    sqmms(kt - 1)
            sqmms(KT - 1)
            t.wait_ge(s_vc, 2)
            t.matmul(psB[:, 0:2], ones_row[0:1, :], scl_sb[0:1, 4:6],
                     start=True, stop=True).then_inc(s_pb, 1)
            t.wait_ge(s_vst, 1)
            t.wait_ge(s_vc, 3)
            t.matmul(psB[:, 0:512], ones_row[0:1, :], srows[0:1, 0:512],
                     start=True, stop=True)
            t.matmul(psB[:, 512:1024], ones_row[0:1, :], srows[0:1, 512:1024],
                     start=True, stop=True).then_inc(s_pb, 1)
            t.wait_ge(s_st, 1)
            t.matmul(psB[:, 1024:1536], ones_row[0:1, :], srows[0:1, 2048:2560],
                     start=True, stop=True)
            t.matmul(psB[:, 1536:2048], ones_row[0:1, :], srows[0:1, 2560:3072],
                     start=True, stop=True).then_inc(s_pb, 1)
            t.wait_ge(*rsem(s_wgp, 0))
            t.wait_ge(*rsem(s_wgp, 1))
            lastg = [None, None]
            for gi, (k0g, klg) in enumerate(zip(XG0, XG)):
                t.wait_ge(s_xnv, gi + 1)
                for kt in range(k0g, k0g + klg):
                    for jt in (0, 1):
                        S = psA if jt == 0 else psB
                        pan = (jt % 2) * KT * 128
                        lhs = gpan[:, pan + kt * 128:pan + (kt + 1) * 128]
                        t.matmul(S[:, 0:512], lhs, xnT[:, kt * NTOK:kt * NTOK + 512],
                                 start=(kt == 0), stop=(kt == KT - 1))
                        lastg[jt] = t.matmul(
                            S[:, 512:1024], lhs,
                            xnT[:, kt * NTOK + 512:kt * NTOK + 1024],
                            start=(kt == 0), stop=(kt == KT - 1))
            lastg[0].then_inc(s_pg, 1)
            lastg[1].then_inc(s_pg, 1)
            for jt in (0, 1):
                S = psA if jt == 0 else psB
                pan = (jt % 2) * KT * 128
                t.wait_ge(*rsem(s_wup, jt))
                for kt in range(KT):
                    lhs = upan[:, pan + kt * 128:pan + (kt + 1) * 128]
                    t.matmul(S[:, 1024:1536], lhs, xnT[:, kt * NTOK:kt * NTOK + 512],
                             start=(kt == 0), stop=(kt == KT - 1))
                    i2 = t.matmul(S[:, 1536:2048], lhs,
                                  xnT[:, kt * NTOK + 512:kt * NTOK + 1024],
                                  start=(kt == 0), stop=(kt == KT - 1))
                i2.then_inc(s_pu, 1)
            for jt in range(2, JT):
                S = psA if jt % 2 == 0 else psB
                pan = (jt % 2) * KT * 128
                t.wait_ge(*rsem(s_wgp, jt))
                t.wait_ge(s_si, jt - 1)
                t.wait_ge(s_hv, jt - 1)
                for kt in range(KT):
                    lhs = gpan[:, pan + kt * 128:pan + (kt + 1) * 128]
                    t.matmul(S[:, 0:512], lhs, xnT[:, kt * NTOK:kt * NTOK + 512],
                             start=(kt == 0), stop=(kt == KT - 1))
                    i2 = t.matmul(S[:, 512:1024], lhs,
                                  xnT[:, kt * NTOK + 512:kt * NTOK + 1024],
                                  start=(kt == 0), stop=(kt == KT - 1))
                i2.then_inc(s_pg, 1)
                t.wait_ge(*rsem(s_wup, jt))
                for kt in range(KT):
                    lhs = upan[:, pan + kt * 128:pan + (kt + 1) * 128]
                    t.matmul(S[:, 1024:1536], lhs, xnT[:, kt * NTOK:kt * NTOK + 512],
                             start=(kt == 0), stop=(kt == KT - 1))
                    i2 = t.matmul(S[:, 1536:2048], lhs,
                                  xnT[:, kt * NTOK + 512:kt * NTOK + 1024],
                                  start=(kt == 0), stop=(kt == KT - 1))
                i2.then_inc(s_pu, 1)
            t.wait_ge(s_si, JT)
            t.wait_ge(s_hv, JT)
            for p in range(NPASS):
                for dt in range(DTP):
                    i = p * DTP + dt
                    S = pslot(dt)
                    t.wait_ge(*rsem(s_wcp, i))
                    if p > 0:
                        t.wait_ge(s_ve, (p - 1) * DTP + dt + 1)
                    pan = (i % 2) * KT * 128
                    for kt in range(KT):
                        lhs = wch[:, pan + kt * 128:pan + (kt + 1) * 128]
                        t.matmul(S[:, 0:512], lhs, xnT[:, kt * NTOK:kt * NTOK + 512],
                                 start=(kt == 0), stop=False)
                        i2 = t.matmul(S[:, 512:1024], lhs,
                                      xnT[:, kt * NTOK + 512:kt * NTOK + 1024],
                                      start=(kt == 0), stop=False)
                    i2.then_inc(s_pa, 1)
                for jt in range(JT):
                    g4, j4 = divmod(jt, 4)
                    q2, j2 = divmod(jt, 2)
                    gidx = p * DG + g4
                    gi = p * HG + q2
                    if j4 == 0:
                        t.wait_ge(*rsem(s_wdp, gidx))
                    if j2 == 0:
                        t.wait_ge(*rsem(s_hlp, gi))
                    dsl = (gidx % DPR) * DTP * 512
                    hsl = (gi % HLR) * 2 * NTOK + j2 * NTOK
                    last = jt == JT - 1
                    for dt in range(DTP):
                        S = pslot(dt)
                        lhs = dpan[:, dsl + (j4 * DTP + dt) * 128:
                                   dsl + (j4 * DTP + dt + 1) * 128]
                        t.matmul(S[:, 0:512], lhs, hld[:, hsl:hsl + 512],
                                 start=False, stop=last)
                        i2 = t.matmul(S[:, 512:1024], lhs, hld[:, hsl + 512:hsl + 1024],
                                      start=False, stop=last)
                        if last:
                            i2.then_inc(s_pde, 1)
                    if j2 == 1 and not last:
                        i2.then_inc(s_pdh, 1)

        @blk.vector
        def _(v):
            v.memset(hld[:, 8192:8832], 1.0).then_inc(s_wrm, 1)
            v.memset(ones_col[:, :], 1.0)
            v.memset(eps_sb[0:1, :], EPS)
            v.memset(ones_row[0:1, :], 1.0).then_inc(s_vc, 1)
            v.wait_ge(s_cst, 32)
            v.tensor_copy(scl_sb[0:1, 4:5], scl_sb[0:1, 0:1])
            v.tensor_mul(scl_sb[0:1, 5:6], scl_sb[0:1, 1:2], scl_sb[0:1, 2:3]).then_inc(s_vc, 1)
            vch = [0]

            def vedge(inst):
                vch[0] += 1
                inst.then_inc(s_vch, 1)
                return vch[0]

            for kt in range(KT):
                v.wait_ge(s_xa[kt // 8], 32)
                if kt >= 8:
                    v.wait_ge(s_pq, kt - 7)
                o = kt * NTOK
                ox = (kt % 8) * NTOK
                v.tensor_mul(hld[:, ox:ox + NTOK], xnT[:, o:o + NTOK],
                             xnT[:, o:o + NTOK]).then_inc(s_sq, 1)
            v.wait_ge(s_px, KT)
            v.wait_ge(s_pq, KT)
            vedge(v.tensor_scalar_mul(srows[0:1, 0:1024], psA[0:1, 0:1024], 1.0 / D))
            n2 = vedge(v.tensor_scalar_mul(srows[0:1, 1024:2048], psA[0:1, 1024:2048], 1.0 / D))
            v.wait_ge(s_vch, n2)
            n3 = vedge(v.tensor_mul(srows[0:1, 2048:3072], srows[0:1, 0:1024], srows[0:1, 0:1024]))
            v.wait_ge(s_vch, n3)
            v.tensor_sub(ost[0:1, 0:1024], srows[0:1, 1024:2048],
                         srows[0:1, 2048:3072]).then_inc(s_vst, 1)
            v.wait_ge(s_pb, 1)
            v.tensor_copy(sgc[:, :], psB[:, 0:1])
            v.tensor_copy(susdc[:, :], psB[:, 1:2]).then_inc(s_vc, 1)
            v.wait_ge(s_pb, 2)
            nmc = vedge(v.tensor_copy(m_bc[:, :], psB[:, 0:1024]))
            v.wait_ge(s_vch, nmc)
            n = vedge(v.tensor_sub(
                hld[:, 0:XG[0] * NTOK].rearrange("p (k n) -> p k n", k=XG[0]),
                xnT[:, 0:XG[0] * NTOK].rearrange("p (k n) -> p k n", k=XG[0]),
                m_bc[:, :].unsqueeze(1).broadcast_to([128, XG[0], NTOK])))
            v.wait_ge(s_pb, 3)
            v.tensor_copy(r_bc[:, :], psB[:, 1024:2048]).then_inc(s_mbc, 1)
            v.wait_ge(s_mbc, 1)
            v.wait_ge(s_vch, n)
            v.tensor_mul(
                xnT[:, 0:XG[0] * NTOK].rearrange("p (k n) -> p k n", k=XG[0]),
                hld[:, 0:XG[0] * NTOK].rearrange("p (k n) -> p k n", k=XG[0]),
                r_bc[:, :].unsqueeze(1).broadcast_to([128, XG[0], NTOK])).then_inc(s_xnv, 1)
            for pr in range(1, len(XG)):
                k0, kl = XG0[pr], XG[pr]
                v.wait_ge(s_xnv, pr)
                n = vedge(v.tensor_sub(
                    hld[:, 0:kl * NTOK].rearrange("p (k n) -> p k n", k=kl),
                    xnT[:, k0 * NTOK:(k0 + kl) * NTOK].rearrange("p (k n) -> p k n", k=kl),
                    m_bc[:, :].unsqueeze(1).broadcast_to([128, kl, NTOK])))
                v.wait_ge(s_vch, n)
                v.tensor_mul(
                    xnT[:, k0 * NTOK:(k0 + kl) * NTOK].rearrange("p (k n) -> p k n", k=kl),
                    hld[:, 0:kl * NTOK].rearrange("p (k n) -> p k n", k=kl),
                    r_bc[:, :].unsqueeze(1).broadcast_to([128, kl, NTOK])).then_inc(s_xnv, 1)
            v.wait_ge(s_vc, 3)
            for jt in range(JT):
                S = psA if jt % 2 == 0 else psB
                v.wait_ge(s_pu, jt + 1)
                v.wait_ge(s_si, jt + 1)
                if jt >= 4:
                    v.wait_ge(*rsem(s_hsp, (jt - 4) // 2))
                v.scalar_tensor_tensor(
                    hsb[:, (jt % 4) * NTOK:(jt % 4 + 1) * NTOK],
                    S[:, 1024:2048], susdc[:, :],
                    silu_t[:, 0:NTOK],
                    ALU.mult, ALU.mult,
                ).then_inc(s_hv, 1)
            for p in range(NPASS):
                for dt in range(DTP):
                    i = p * DTP + dt
                    gxt = 2 * KT + i
                    v.wait_ge(s_pde, p * DTP + dt + 1)
                    v.wait_ge(*rsem(s_xgp, i))
                    if i in (30, 31):
                        if i == 31:
                            v.wait_ge(s_ve, 29)
                        dst = silu_t[:, 0:NTOK] if i == 30 else xt[:, 0:NTOK]
                    else:
                        if i >= 2:
                            v.wait_ge(*rsem(s_osp, i - 2))
                        dst = ost[:, (i % 2) * NTOK:(i % 2 + 1) * NTOK]
                    v.scalar_tensor_tensor(
                        dst,
                        pslot(dt), bias_sb[:, p * DTP + dt:p * DTP + dt + 1],
                        xt[:, (gxt % 4) * NTOK:(gxt % 4 + 1) * NTOK],
                        ALU.add, ALU.add,
                    ).then_inc(s_ve, 1)

        @blk.scalar
        def _(s):
            s.wait_ge(s_vst, 1)
            s.wait_ge(s_vc, 1)
            s.activation(srows[0:1, 1024:2048], ost[0:1, 0:1024],
                         AF.Ln, bias=eps_sb[0:1, :]).then_inc(s_sq2, 1)
            s.wait_ge(s_sq2, 1)
            s.activation(srows[0:1, 2048:3072], srows[0:1, 1024:2048],
                         AF.Exp, scale=-0.5).then_inc(s_st, 1)
            s.wait_ge(s_vc, 3)
            for jt in range(JT):
                S = psA if jt % 2 == 0 else psB
                s.wait_ge(s_pg, jt + 1)
                if jt >= 1:
                    s.wait_ge(s_hv, jt)
                s.activation(silu_t[:, 0:NTOK],
                             S[:, 0:1024], AF.Silu, scale=sgc[:, :]).then_inc(s_si, 1)
                if jt % 2 == 1:
                    q = jt // 2
                    s.wait_ge(s_hv, jt + 1)
                    s.dma_start(
                        hbf[2 * q:2 * q + 2].rearrange("j p n -> p j n"),
                        hsb[:, ((2 * q) % 4) * NTOK:((2 * q) % 4 + 2) * NTOK]
                        .rearrange("p (j n) -> p j n", j=2),
                    ).then_inc(s_hsp[q % 2], 16)
            for p in range(NPASS):
                for dt in range(DTP):
                    i = p * DTP + dt
                    s.wait_ge(s_ve, i + 1)
                    r0 = (p * DTP + dt) * 128
                    if i == 30:
                        src = silu_t[:, 0:NTOK]
                    elif i == 31:
                        src = xt[:, 0:NTOK]
                    else:
                        src = ost[:, (i % 2) * NTOK:(i % 2 + 1) * NTOK]
                    s.dma_start(
                        yt[r0:r0 + 128, :], src,
                    ).then_inc(s_osp[i % 2], 16)
            s.wait_ge(s_osp[0], 16 * (NPASS * DTP // 2))
            s.wait_ge(s_osp[1], 16 * (NPASS * DTP // 2))

    return nc


_NC_CACHE = {}


def _get_program():
    if "nc" not in _NC_CACHE:
        _NC_CACHE["nc"] = build_program()
    return _NC_CACHE["nc"]


def kernel(L0_attn_x, L0_attn_weight, L0_attn_bias, L0_ffn_gate_w, L0_ffn_gate_s,
           L0_ffn_up_w, L0_ffn_up_s, L0_ffn_down_w, L0_ffn_down_s):
    x = np.ascontiguousarray(np.asarray(L0_attn_x, dtype=np.float32).reshape(NCORES * NTOK, D))
    W = np.asarray(L0_attn_weight, dtype=np.float32)
    b = np.asarray(L0_attn_bias, dtype=np.float32)
    G = np.asarray(L0_ffn_gate_w, dtype=np.int32)
    U = np.asarray(L0_ffn_up_w, dtype=np.int32)
    Dw = np.asarray(L0_ffn_down_w, dtype=np.int32)

    wt = np.ascontiguousarray(W.reshape(NPASS, DTP, 128, KT, 128).transpose(0, 1, 4, 3, 2))
    gph = np.ascontiguousarray(G.reshape(JT, 128, KT, 128).transpose(0, 3, 2, 1))
    uph = np.ascontiguousarray(U.reshape(JT, 128, KT, 128).transpose(0, 3, 2, 1))
    dwh = np.ascontiguousarray(Dw.reshape(NPASS, DTP, 128, JT, 128).transpose(0, 3, 4, 1, 2))
    bsh = np.ascontiguousarray(b.reshape(KT, 128).T)
    sclh = np.ascontiguousarray(np.stack(
        [np.asarray(L0_ffn_gate_s, np.float32).reshape(()),
         np.asarray(L0_ffn_up_s, np.float32).reshape(()),
         np.asarray(L0_ffn_down_s, np.float32).reshape(())]).reshape(1, 3))

    nc = _get_program()
    in_maps = []
    for c in range(NCORES):
        xf_c = np.ascontiguousarray(x[c * NTOK:(c + 1) * NTOK, :].T)
        in_maps.append({"xf": xf_c, "wt": wt, "gp": gph, "up": uph,
                        "dw": dwh, "bs": bsh, "scl": sclh})

    res = run_bass_kernel_spmd(nc, in_maps, core_ids=list(range(NCORES)))

    out = np.empty((NCORES * NTOK, D), dtype=np.float32)
    for c in range(NCORES):
        out[c * NTOK:(c + 1) * NTOK, :] = res.results[c]["yt"].T
    return out.reshape(4, 2048, D)
